# version 01 (final)
# baseline (speedup 1.0000x reference)
import os
os.environ.setdefault("TRN_ALLOW_ICI_ALL_TO_ALL", "1")
import numpy as np
import ml_dtypes
import concourse.bass as bass
import concourse.mybir as mybir
from concourse.alu_op_type import AluOpType as AL
from concourse.bass_utils import run_bass_kernel_spmd

F32 = mybir.dt.float32
BF16 = mybir.dt.bfloat16
F16 = mybir.dt.float16
I32 = mybir.dt.int32
U8 = mybir.dt.uint8
AF = mybir.ActivationFunctionType
AX = mybir.AxisListType

NCORE = 8
BPC = 2
S = 2048
D = 1024
H = 16
NT = 16
DI = 2048
NHEADS = 32
PHD = 64
QCH = 128
NCHUNK = 16

_DBG = []
_PHASES = 4


class Bld:
    ENGS = ("sync", "vector", "scalar", "tensor", "gpsimd")

    def __init__(self):
        self.ops = {e: [] for e in self.ENGS}
        self.cnt = {e: 0 for e in self.ENGS}

    def op(self, eng, fn, waits=()):
        inc = 16 if eng == "sync" else 1
        w = {}
        for ev in waits:
            if ev is None:
                continue
            se, v = ev
            if v <= 0:
                continue
            w[se] = max(w.get(se, 0), v)
        self.ops[eng].append((fn, w, inc))
        self.cnt[eng] += inc
        return (eng, self.cnt[eng])

    def dma(self, fn, waits=()):
        return self.op("sync", fn, waits)

    def dmat(self, fn, waits=()):
        drain = ("sync", self.cnt["sync"])
        ev = self.op("sync", fn, list(waits) + [drain])
        self.ops["sync"].append((None, {"sync": ev[1]}, 0))
        return ev

    def gp(self, fn, waits=()):
        return self.op("gpsimd", fn, waits)

    def ev(self, eng):
        return (eng, self.cnt[eng])

    def barrier(self):
        evs = {e: self.cnt[e] for e in self.ENGS}
        out = []
        for e in self.ENGS:
            w = {se: v for se, v in evs.items() if v > 0}
            self.ops[e].append((None, w, 0))
            out.append((e, evs[e]))
        return out

    def finalize(self, block, sems):
        def mk(engname):
            def body(ectx):
                waited = {}
                for fn, w, inc in self.ops[engname]:
                    for se, v in w.items():
                        if waited.get(se, 0) >= v:
                            continue
                        ectx.wait_ge(sems[se], v)
                        waited[se] = v
                    if fn is None:
                        continue
                    inst = fn(ectx)
                    if inc:
                        inst.then_inc(sems[engname], inc)
            return body
        for eng in self.ENGS:
            if self.ops[eng]:
                getattr(block, eng)(mk(eng))


class Arena:
    def __init__(self, t, nbytes):
        self.t = t
        self.n = nbytes
        self.off = 0
        self.stack = []

    def mark(self):
        self.stack.append(self.off)

    def reset(self):
        self.off = self.stack.pop()

    def alloc(self, dtype, cols):
        size = {F32: 4, BF16: 2, F16: 2, I32: 4}[dtype]
        nb = (cols * size + 31) // 32 * 32
        assert self.off + nb <= self.n, f"arena overflow {self.off + nb} > {self.n}"
        off = self.off
        self.off += nb
        return self.t[:, off:off + cols * size].bitcast(dtype)


class Psum:
    def __init__(self, tiles):
        self.t = tiles
        self.free_ev = [None] * len(tiles)
        self.rr = 0

    def acquire(self):
        i = self.rr
        self.rr = (self.rr + 1) % len(self.t)
        return i, self.t[i], self.free_ev[i]

    def release(self, i, ev):
        self.free_ev[i] = ev


def build_nc(dbg_names=()):
    nc = bass.Bass("TRN2", target_bir_lowering=False, debug=False)
    keep = []
    dt = {}

    def din(name, shape, dtype=F32):
        dt[name] = nc.dram_tensor(name, shape, dtype, kind="ExternalInput").ap()
        return dt[name]

    def dscr(name, shape, dtype):
        dt[name] = nc.dram_tensor(name, shape, dtype).ap()
        return dt[name]

    din("x0", [BPC, S, D])
    for w in ("wqT", "wkT", "wvT", "woT", "wfT", "ufT", "wiT", "uiT", "wcT", "ucT", "wo2T", "uoT"):
        din(w, [D, D])
    din("bfc", [128, 8]); din("bic", [128, 8]); din("bcc", [128, 8])
    din("bor", [1, D]); din("incc", [128, 8])
    din("winT", [D, 4384])
    din("convw", [128, 72]); din("convb", [128, 18])
    din("dtb", [32, 1]); din("alog", [32, 1]); din("dvec", [1, 32])
    din("normw", [128, 16])
    din("woutT", [DI, D])
    din("w1T", [D, 4096]); din("b1c", [128, 32])
    din("w2T", [4096, D]); din("b2r", [1, D])
    din("onesr", [1, 128])
    din("sltA", [128, 128])
    din("triuI", [128, 128])
    din("sel127", [128, 128])
    din("slnegA", [128, 128])
    din("identw", [128, 128])
    y_out = nc.dram_tensor("y", [BPC, S, D], F32, kind="ExternalOutput").ap()

    for w in ("ternq", "ternk", "ternv", "terno"):
        dscr("s_" + w, [D, D], BF16)
    dscr("s_tern1", [D, 4096], BF16)
    dscr("s_tern2", [4096, D], BF16)
    for w in ("wfT", "ufT", "wiT", "uiT", "wcT", "ucT", "wo2T", "uoT"):
        dscr("s_" + w, [D, D], BF16)
    dscr("s_win", [D, 4384], BF16)
    dscr("s_wout", [DI, D], BF16)
    dscr("s_x1", [BPC, S, D], F32)
    dscr("s_x2", [BPC, S, D], F32)
    dscr("s_x3", [BPC, S, D], F32)
    dscr("s_xln1", [4, BPC, 512, D], BF16)
    dscr("a2a_in", [NCORE, BPC, 256, D], BF16)
    dscr("a2a_out", [NCORE, BPC, 256, D], BF16)
    dscr("s_cfin", [D, 256], BF16)
    dscr("ag_cf", [NCORE, D, 256], BF16)
    dscr("s_xln2", [BPC, S, D], BF16)
    dscr("s_xln3", [BPC, S, D], BF16)
    dscr("s_ctx", [BPC, S, D], BF16)
    dscr("s_z", [BPC, S, DI], BF16)
    dscr("s_yn", [BPC, S, DI], BF16)
    dscr("s_xs", [BPC, DI, S], BF16)

    taps = {}

    def tap_dram(name, src_ap, shape, dtype, waits):
        if name not in dbg_names:
            return
        t = nc.dram_tensor("dbg_" + name, shape, dtype, kind="ExternalOutput").ap()
        taps[name] = t
        b.dma(lambda e: e.dma_start(out=t, in_=src_ap), waits)

    def tap_sb(name, mk_src, shape, dtype, waits):
        if name not in dbg_names:
            return
        t = nc.dram_tensor("dbg_" + name, shape, dtype, kind="ExternalOutput").ap()
        taps[name] = t
        ev = b.dma(lambda e: e.dma_start(out=t, in_=mk_src()), waits)
        b.op("vector", lambda e: e.nop(), [ev])
        b.op("scalar", lambda e: e.nop(), [ev])
        b.op("tensor", lambda e: e.nop(), [ev])

    def sb(name, shape, dtype):
        cm = nc.sbuf_tensor(name, shape, dtype)
        keep.append(cm)
        return cm.__enter__()

    ARENA_BYTES = 181 * 1024
    arena_t = sb("arena", [128, ARENA_BYTES], U8)
    fb_a = sb("fb_a", [1, 64], U8)
    fb_b = sb("fb_b", [1, 64], U8)

    def fence(waits=()):
        return b.dma(lambda e: e.dma_start(out=fb_b[0:1, :], in_=fb_a[0:1, :]), waits)
    onescol = sb("onescol", [128, 1], F32)
    onesr_f = sb("onesr_f", [1, 128], F32)
    onesr_b = sb("onesr_b", [1, 128], BF16)
    slneg_b = sb("slneg_b", [128, 128], BF16)
    ident_b = sb("ident_b", [128, 128], BF16)
    slneg_f = sb("slneg_f", [128, 128], F32)
    scal = sb("scal", [1, 32], F32)
    mwcols = sb("mwcols", [128, 8], F32)
    biasf = sb("biasf", [128, 8], F32)
    biasi = sb("biasi", [128, 8], F32)
    biasc = sb("biasc", [128, 8], F32)
    incc_s = sb("incc_s", [128, 8], F32)
    b1sb = sb("b1sb", [128, 32], F32)
    dtb_s = sb("dtb_s", [32, 1], F32)
    nega_s = sb("nega_s", [32, 1], F32)
    dvec_s = sb("dvec_s", [1, 32], F32)
    normw_s = sb("normw_s", [128, 16], F32)
    convw_s = sb("convw_s", [128, 72], F32)
    convb_s = sb("convb_s", [128, 18], F32)
    bo_b = sb("bo_b", [1, D], BF16)
    b2_b = sb("b2_b", [1, D], BF16)
    bor_f = sb("bor_f", [1, D], F32)
    b2r_f = sb("b2r_f", [1, D], F32)
    junk = sb("junk", [128, 4096], BF16)
    zero32 = sb("zero32", [32, QCH], F32)

    pst = []
    for i in range(8):
        cm = nc.psum_tensor(f"ps{i}", [128, 512], F32)
        keep.append(cm)
        pst.append(cm.__enter__())
    ps = Psum(pst)

    sems = {}
    for e in Bld.ENGS:
        cm = nc.semaphore("sem_" + e)
        keep.append(cm)
        sems[e] = cm.__enter__()

    b = Bld()
    A = Arena(arena_t, ARENA_BYTES)

    MAXNEG = -1e30

    evc = []
    evc.append(b.dma(lambda e: e.dma_start(out=onesr_f[:, :], in_=dt["onesr"][:, :])))
    evc.append(b.dma(lambda e: e.dma_start(out=onescol[:, :], in_=dt["onesr"].rearrange("a b -> b a"))))
    evc.append(b.dma(lambda e: e.dma_start(out=biasf[:, :], in_=dt["bfc"][:, :])))
    evc.append(b.dma(lambda e: e.dma_start(out=biasi[:, :], in_=dt["bic"][:, :])))
    evc.append(b.dma(lambda e: e.dma_start(out=biasc[:, :], in_=dt["bcc"][:, :])))
    evc.append(b.dma(lambda e: e.dma_start(out=incc_s[:, :], in_=dt["incc"][:, :])))
    evc.append(b.dma(lambda e: e.dma_start(out=b1sb[:, :], in_=dt["b1c"][:, :])))
    evc.append(b.dma(lambda e: e.dma_start(out=dtb_s[:, :], in_=dt["dtb"][:, :])))
    evc.append(b.dma(lambda e: e.dma_start(out=dvec_s[:, :], in_=dt["dvec"][:, :])))
    evc.append(b.dma(lambda e: e.dma_start(out=normw_s[:, :], in_=dt["normw"][:, :])))
    evc.append(b.dma(lambda e: e.dma_start(out=convw_s[:, :], in_=dt["convw"][:, :])))
    evc.append(b.dma(lambda e: e.dma_start(out=convb_s[:, :], in_=dt["convb"][:, :])))
    evc.append(b.dma(lambda e: e.dma_start(out=bor_f[:, :], in_=dt["bor"][:, :])))
    evc.append(b.dma(lambda e: e.dma_start(out=b2r_f[:, :], in_=dt["b2r"][:, :])))
    ev_alog = b.dma(lambda e: e.dma_start(out=nega_s[:, :], in_=dt["alog"][:, :]))
    ev_slA = b.dma(lambda e: e.dma_start(out=slneg_f[:, :], in_=dt["slnegA"][:, :]))
    ev_id = b.dma(lambda e: e.dma_start(out=junk[:, 0:256].bitcast(F32), in_=dt["identw"][:, :]))
    evc_f = fence()
    evc = [evc_f] * len(evc)
    ev_alog = ev_slA = ev_id = evc_f
    ev_onb = b.op("vector", lambda e: e.tensor_copy(onesr_b[:, :], onesr_f[:, :]), [evc[0]])
    ev_slb = b.op("vector", lambda e: e.tensor_copy(slneg_b[:, :], slneg_f[:, :]), [ev_slA])
    ev_idb = b.op("vector", lambda e: e.tensor_copy(ident_b[:, :], junk[:, 0:256].bitcast(F32)), [ev_id])
    ev_bob = b.op("vector", lambda e: e.tensor_copy(bo_b[:, :], bor_f[:, :]), [evc[12]])
    ev_b2b = b.op("vector", lambda e: e.tensor_copy(b2_b[:, :], b2r_f[:, :]), [evc[13]])
    ev_na1 = b.op("scalar", lambda e: e.activation(nega_s[:, :], nega_s[:, :], AF.Exp), [ev_alog])
    ev_na = b.op("vector", lambda e: e.tensor_scalar(nega_s[:, :], nega_s[:, :], -1.0, None, AL.mult, AL.bypass), [ev_na1])
    ev_z32 = b.op("vector", lambda e: e.memset(zero32[:, :], 0.0))
    def prep_tern(wname, sname, out_cols, nrows, eps, mw_slot=None):
        nrc = nrows // 128
        guard = [b.ev("vector"), b.ev("scalar"), b.ev("tensor")]
        A.mark()
        wt = A.alloc(F32, nrc * out_cols)
        c2 = A.alloc(F32, out_cols)
        tb = A.alloc(BF16, out_cols)
        asum = A.alloc(F32, 33)
        ld = []
        for r in range(nrc):
            ld.append(b.dma(lambda e, r=r: e.dma_start(
                out=wt[:, r * out_cols:(r + 1) * out_cols],
                in_=dt[wname][r * 128:(r + 1) * 128, :]), guard))
        ldf = fence()
        ev_ab = []
        for r in range(nrc):
            ev_ab.append(b.op("scalar", lambda e, r=r: e.activation(
                junk[:, 0:out_cols], wt[:, r * out_cols:(r + 1) * out_cols], AF.Abs,
                accum_out=asum[:, r:r + 1]), [ldf]))
        ev_rs = b.op("vector", lambda e: e.tensor_reduce(
            asum[:, 32:33], asum[:, 0:nrc], AX.X, AL.add), ev_ab)
        pi, pt, fe = ps.acquire()
        ev_tot = b.op("tensor", lambda e, p=pt: e.matmul(
            p[0:1, 0:1], asum[:, 32:33], onescol[:, :], start=True, stop=True),
            [ev_rs, fe, evc[1]])
        ev_s1 = b.op("vector", lambda e, p=pt: e.tensor_scalar(
            scal[0:1, 30:31], p[0:1, 0:1], 1.0 / (nrows * out_cols), eps, AL.mult, AL.max), [ev_tot])
        ps.release(pi, ev_s1)
        ev_s2 = b.op("vector", lambda e: e.tensor_scalar(
            scal[0:1, 28:29], scal[0:1, 30:31], 0.5, None, AL.mult, AL.bypass), [ev_s1])
        ev_s3 = b.op("vector", lambda e: e.tensor_scalar(
            scal[0:1, 29:30], scal[0:1, 28:29], -1.0, None, AL.mult, AL.bypass), [ev_s2])
        if mw_slot is not None:
            b.op("vector", lambda e: e.tensor_copy(scal[0:1, mw_slot:mw_slot + 1], scal[0:1, 30:31]), [ev_s1])
        pi2, pt2, fe2 = ps.acquire()
        ev_b1 = b.op("tensor", lambda e, p=pt2: e.matmul(
            p[:, 0:2], onesr_f[:, :], scal[0:1, 28:30], start=True, stop=True),
            [ev_s3, fe2, evc[0]])
        thr2 = A.alloc(F32, 2)
        ev_thr = b.op("vector", lambda e, p=pt2: e.tensor_copy(thr2[:, 0:2], p[:, 0:2]), [ev_b1])
        ps.release(pi2, ev_thr)
        if sname == "s_ternk":
            tap_sb("asumk", lambda: asum[:, 0:33], [128, 33], F32, [ev_rs])
            tap_sb("thrk", lambda: thr2[:, 0:2], [128, 2], F32, [ev_thr])
        ev_last = None
        prev_store = None
        for r in range(nrc):
            e1 = b.op("vector", lambda e, r=r: e.tensor_scalar(
                c2[:, :], wt[:, r * out_cols:(r + 1) * out_cols], thr2[:, 1:2], None,
                AL.is_lt, AL.bypass), [ev_thr])
            e2 = b.op("vector", lambda e, r=r: e.scalar_tensor_tensor(
                tb[:, :], wt[:, r * out_cols:(r + 1) * out_cols], thr2[:, 0:1], c2[:, :],
                AL.is_gt, AL.subtract), [e1, prev_store])
            ev_last = b.dma(lambda e, r=r: e.dma_start(
                out=dt[sname][r * 128:(r + 1) * 128, :], in_=tb[:, :]), [e2])
            prev_store = ev_last
        A.reset()
        return ev_last

    ev_tq = prep_tern("wqT", "s_ternq", D, D, 1e-5, mw_slot=0)
    ev_tk = prep_tern("wkT", "s_ternk", D, D, 1e-5, mw_slot=1)
    ev_tv = prep_tern("wvT", "s_ternv", D, D, 1e-5, mw_slot=2)
    ev_to = prep_tern("woT", "s_terno", D, D, 1e-5, mw_slot=3)
    tap_dram("tq", dt["s_ternq"], [D, D], BF16, [ev_tq])
    tap_dram("tk", dt["s_ternk"], [D, D], BF16, [ev_tk])
    tap_dram("tv", dt["s_ternv"], [D, D], BF16, [ev_tv])
    tap_dram("to", dt["s_terno"], [D, D], BF16, [ev_to])
    ev_sq8 = b.op("vector", lambda e: e.tensor_scalar(
        scal[0:1, 0:2], scal[0:1, 0:2], 1.0 / (8.0 ** 0.5), None, AL.mult, AL.bypass), [b.ev("vector")])
    pi3, pt3, fe3 = ps.acquire()
    ev_mwb = b.op("tensor", lambda e, p=pt3: e.matmul(
        p[:, 0:4], onesr_f[:, :], scal[0:1, 0:4], start=True, stop=True), [ev_sq8, fe3])
    ev_mwc = b.op("vector", lambda e, p=pt3: e.tensor_copy(mwcols[:, 0:4], p[:, 0:4]), [ev_mwb])
    ps.release(pi3, ev_mwc)
    tap_sb("mwc", lambda: mwcols[:, 0:4], [128, 4], F32, [ev_mwc])

    def prep_cast(wname, sname, ncols, nrows, fold_normw=False):
        nrc = nrows // 128
        guard = [b.ev("vector"), b.ev("scalar"), b.ev("tensor")]
        A.mark()
        wt = A.alloc(F32, ncols)
        tb = A.alloc(BF16, ncols)
        wt2 = A.alloc(F32, ncols)
        tb2 = A.alloc(BF16, ncols)
        ev_last = None
        bufs = [(wt, tb), (wt2, tb2)]
        ev_st = [None, None]
        for r in range(nrc):
            wtb, tbb = bufs[r % 2]
            evl = b.dma(lambda e, r=r, w=wtb: e.dma_start(
                out=w[:, :], in_=dt[wname][r * 128:(r + 1) * 128, :]),
                [ev_st[r % 2]] + guard)
            evl = fence()
            if fold_normw:
                evx = b.op("vector", lambda e, r=r, w=wtb, t=tbb: e.tensor_scalar(
                    t[:, :], w[:, :], normw_s[:, r:r + 1], None, AL.mult, AL.bypass),
                    [evl, evc[9]])
            else:
                evx = b.op("scalar", lambda e, w=wtb, t=tbb: e.activation(
                    t[:, :], w[:, :], AF.Identity), [evl])
            ev_last = b.dma(lambda e, r=r, t=tbb: e.dma_start(
                out=dt[sname][r * 128:(r + 1) * 128, :], in_=t[:, :]), [evx])
            ev_st[r % 2] = ev_last
        A.reset()
        return ev_last

    ev_dense = []
    for w in ("wfT", "ufT", "wiT", "uiT", "wcT", "ucT", "wo2T", "uoT"):
        ev_dense.append(prep_cast(w, "s_" + w, D, D))
    ev_win = prep_cast("winT", "s_win", 4384, D)
    ev_wout = prep_cast("woutT", "s_wout", D, DI, fold_normw=True)
    ev_t1 = prep_tern("w1T", "s_tern1", 4096, D, 1e-8)
    ev_t2 = prep_tern("w2T", "s_tern2", D, 4096, 1e-8)
    b.barrier()

    x0t = A.alloc(F32, D)
    xln = A.alloc(F32, D)
    d_t = A.alloc(F32, D)
    tmp_t = A.alloc(F32, D)
    c2_t = A.alloc(F32, D)
    ti32 = A.alloc(I32, D)
    lnt = A.alloc(F32, D)
    stats = A.alloc(F32, 64)
    ln_bf = A.alloc(BF16, D)
    big1 = A.alloc(F32, 2048)
    big2 = A.alloc(F32, 2048)
    big3 = A.alloc(F32, 2048)

    A.mark()
    xi = A.alloc(BF16, NT * D)
    xiT = A.alloc(BF16, 8 * S)
    ternA = A.alloc(BF16, 8 * D)
    ternB = A.alloc(BF16, 8 * D)
    ternC = A.alloc(BF16, 8 * D)
    qkv = A.alloc(BF16, 6 * D)
    SCt = A.alloc(F32, 256)
    Et = A.alloc(F16, 256)
    dent = A.alloc(F32, 16)
    rect = A.alloc(F32, 16)
    ctxr = A.alloc(F32, 64)
    alab = A.alloc(F32, NT * 4)
    ctok = xi
    cqi = xi
    cqiT = xiT
    aot = xln
    prodS = ti32.bitcast(F16)[:, 0:D]
    prodC = ti32.bitcast(F16)[:, D:2 * D]

    def emit_ln(src_view, cols, dst_view, waits, eps=1e-5):
        e_su = b.op("vector", lambda e: e.tensor_reduce(
            stats[:, 24:25], src_view[:, 0:cols], AX.X, AL.add), waits)
        e_mn = b.op("vector", lambda e: e.tensor_scalar(
            stats[:, 12:13], stats[:, 24:25], 1.0 / cols, None, AL.mult, AL.bypass), [e_su])
        e_sq = b.op("scalar", lambda e: e.activation(
            junk[:, 0:cols], src_view[:, 0:cols], AF.Square, accum_out=stats[:, 19:20]), waits)
        e_ms = b.op("vector", lambda e: e.tensor_scalar(
            stats[:, 20:21], stats[:, 19:20], 1.0 / cols, None, AL.mult, AL.bypass), [e_sq])
        e_m2 = b.op("vector", lambda e: e.tensor_tensor(
            stats[:, 25:26], stats[:, 12:13], stats[:, 12:13], AL.mult), [e_mn])
        e_va = b.op("vector", lambda e: e.scalar_tensor_tensor(
            stats[:, 14:15], stats[:, 20:21], eps, stats[:, 25:26],
            AL.add, AL.subtract), [e_ms, e_m2])
        e3 = b.op("vector", lambda e: e.reciprocal(stats[:, 15:16], stats[:, 14:15]), [e_va])
        e4 = b.op("scalar", lambda e: e.activation(stats[:, 15:16], stats[:, 15:16], AF.Sqrt), [e3])
        e5 = b.op("vector", lambda e: e.tensor_scalar(
            dst_view[:, 0:cols], src_view[:, 0:cols], stats[:, 12:13], stats[:, 15:16],
            AL.subtract, AL.mult), [e4] + list(waits))
        return e5

    def emit_quant(src_view, cols, int_dst, waits, alpha_dst=None):
        e_mx = b.op("vector", lambda e: e.tensor_reduce(
            stats[:, 16:17], src_view[:, 0:cols], AX.X, AL.max, apply_absolute_value=True), waits)
        e_mc = b.op("vector", lambda e: e.tensor_scalar(
            stats[:, 16:17], stats[:, 16:17], 1e-30, None, AL.max, AL.bypass), [e_mx])
        e_rm = b.op("vector", lambda e: e.reciprocal(stats[:, 17:18], stats[:, 16:17]), [e_mc])
        e_s = b.op("vector", lambda e: e.tensor_scalar(
            stats[:, 18:19], stats[:, 17:18], 127.0, None, AL.mult, AL.bypass), [e_rm])
        e_sq = b.op("scalar", lambda e: e.activation(
            junk[:, 0:cols], src_view[:, 0:cols], AF.Square, accum_out=stats[:, 19:20]), waits)
        e_ms = b.op("vector", lambda e: e.tensor_scalar(
            stats[:, 20:21], stats[:, 19:20], 1.0 / cols, 1e-6, AL.mult, AL.add), [e_sq])
        e_rr = b.op("vector", lambda e: e.reciprocal(stats[:, 21:22], stats[:, 20:21]), [e_ms])
        e_r = b.op("scalar", lambda e: e.activation(stats[:, 21:22], stats[:, 21:22], AF.Sqrt), [e_rr])
        ev_a = None
        if alpha_dst is not None:
            e_a1 = b.op("vector", lambda e: e.tensor_tensor(
                stats[:, 22:23], stats[:, 16:17], stats[:, 21:22], AL.mult), [e_r, e_mc])
            ev_a = b.op("vector", lambda e: e.tensor_scalar(
                alpha_dst, stats[:, 22:23], 1.0 / 127.0, None, AL.mult, AL.bypass), [e_a1])
        e_i = b.op("scalar", lambda e: e.activation(
            ti32[:, 0:cols], src_view[:, 0:cols], AF.Identity, scale=stats[:, 18:19]),
            [e_s] + list(waits))
        e_d = b.op("vector", lambda e: e.scalar_tensor_tensor(
            d_t[:, 0:cols], src_view[:, 0:cols], stats[:, 18:19], ti32[:, 0:cols],
            AL.mult, AL.subtract), [e_i])
        e_t = b.op("vector", lambda e: e.scalar_tensor_tensor(
            tmp_t[:, 0:cols], d_t[:, 0:cols], 0.5, ti32[:, 0:cols], AL.is_gt, AL.add), [e_d])
        e_c = b.op("vector", lambda e: e.tensor_scalar(
            c2_t[:, 0:cols], d_t[:, 0:cols], -0.5, None, AL.is_lt, AL.bypass), [e_d])
        e_o = b.op("vector", lambda e: e.tensor_tensor(
            int_dst, tmp_t[:, 0:cols], c2_t[:, 0:cols], AL.subtract), [e_t, e_c])
        return e_o, ev_a

    x1_done = []
    xln1_q_done = [[] for _ in range(4)]
    tern_evs = {"s_ternq": ev_tq, "s_ternk": ev_tk, "s_ternv": ev_tv, "s_terno": ev_to}
    prev_ctxq_readers = []
    prev_terno_mm = None
    prev_a5_write = []

    prev_cqiT = None
    for bb in range(BPC):
        if prev_cqiT is not None:
            b.op("vector", lambda e: e.nop(), [prev_cqiT])
        ev_tld = {}
        for sname, tbuf in (("s_ternq", ternA), ("s_ternk", ternB), ("s_ternv", ternC)):
            evs = []
            for r in range(8):
                evs.append(b.dma(lambda e, r=r, sname=sname, tbuf=tbuf: e.dma_start(
                    out=tbuf[:, r * D:(r + 1) * D],
                    in_=dt[sname][r * 128:(r + 1) * 128, :]),
                    [tern_evs[sname], prev_terno_mm] + prev_ctxq_readers))
            ev_tld[sname] = evs
        ev_tldf = fence()

        ev_xi = []
        for tt in range(NT):
            evl = b.dma(lambda e, bb=bb, tt=tt: e.dma_start(
                out=x0t[:, :], in_=dt["x0"][bb, tt * 128:(tt + 1) * 128, :]),
                [b.ev("vector")])
            evl = fence()
            e_ln = emit_ln(x0t, D, xln, [evl])
            if bb == 0 and tt == 0:
                tap_sb("x0t0", lambda: x0t[:, 0:D], [128, D], F32, [e_ln])
                tap_sb("xln0", lambda: xln[:, 0:D], [128, D], F32, [e_ln])
                tap_sb("stats0", lambda: stats[:, 0:32], [128, 32], F32, [e_ln])
            e_q, e_a = emit_quant(xln, D, xi[:, tt * D:(tt + 1) * D], [e_ln],
                                  alpha_dst=alab[:, tt * 4:tt * 4 + 1])
            for j, slot in ((0, 1), (1, 2), (2, 3)):
                b.op("vector", lambda e, tt=tt, j=j, slot=slot: e.tensor_tensor(
                    alab[:, tt * 4 + slot:tt * 4 + slot + 1],
                    alab[:, tt * 4:tt * 4 + 1], mwcols[:, j:j + 1], AL.mult), [e_a, ev_mwc])
            ev_xi.append((e_q, b.ev("vector")))
        if bb == 0:
            tap_sb("xi0", lambda: xi[:, 0:D], [128, D], BF16, [ev_xi[0][0]])

        ev_xiT0 = b.dmat(lambda e: e.dma_start_transpose(
            xiT[:, :].rearrange("p (m r) -> p m r", r=128), xi[:, :]),
            [ev for ev, _ in ev_xi] + [b.ev("tensor")])
        ev_xiT = fence()

        ev_ctx_w = [[] for _ in range(NT)]
        ev_qkv_read = [[], []]
        ev_a3_last = None
        for tt in range(NT):
            par = tt % 2
            base = par * 3 * D
            qsl = qkv[:, base:base + D]
            ksl = qkv[:, base + D:base + 2 * D]
            vsl = qkv[:, base + 2 * D:base + 3 * D]
            ev_pj = []
            for pj, (tbuf, dst, slot) in enumerate(
                    ((ternA, qsl, 1), (ternB, ksl, 2), (ternC, vsl, 3))):
                sname = ("s_ternq", "s_ternk", "s_ternv")[pj]
                for et in range(2):
                    pi, pt, fe = ps.acquire()
                    mme = None
                    for dc in range(8):
                        mme = b.op("tensor", lambda e, p=pt, dc=dc, tt=tt, et=et, tbuf=tbuf: e.matmul(
                            p[:, 0:512],
                            xiT[:, (tt * 8 + dc) * 128:(tt * 8 + dc + 1) * 128],
                            tbuf[:, dc * D + et * 512: dc * D + (et + 1) * 512],
                            start=(dc == 0), stop=(dc == 7)),
                            [ev_xiT, ev_tldf, fe] if dc == 0 else [])
                    ev = b.op("scalar", lambda e, p=pt, dst=dst, tt=tt, et=et, slot=slot: e.activation(
                        dst[:, et * 512:(et + 1) * 512], p[:, 0:512],
                        AF.Identity, scale=alab[:, tt * 4 + slot:tt * 4 + slot + 1]),
                        [mme, ev_xi[tt][1]] + ev_qkv_read[par])
                    ps.release(pi, ev)
                    ev_pj.append(ev)
                    ev_a3_last = mme
            ev_qkv_read[par] = []
            if bb == 0 and tt == 0:
                tap_sb("q0", lambda: qkv[:, 0:D], [128, D], BF16, [ev_pj[0], ev_pj[1]])
                tap_sb("xiT0", lambda: xiT[:, 0:2048], [128, 2048], BF16, [ev_xiT])
                tap_sb("ternq0", lambda: ternA[:, 0:D], [128, D], BF16, [ev_tld["s_ternq"][0]])
                tap_sb("alab0", lambda: alab[:, 0:8], [128, 8], F32, [ev_pj[0]])
            e_sc = []
            for h in range(H):
                e_p = b.op("vector", lambda e, h=h, qsl=qsl, ksl=ksl: e.tensor_tensor(
                    prodS.rearrange("p (t d) -> p t d", t=H),
                    ksl.rearrange("p (t d) -> p t d", t=H),
                    qsl[:, h * 64:(h + 1) * 64].unsqueeze(1).broadcast_to([128, H, 64]),
                    AL.mult), ev_pj)
                e_sc.append(b.op("vector", lambda e, h=h: e.tensor_reduce(
                    SCt[:, h * 16:(h + 1) * 16],
                    prodS.rearrange("p (t d) -> p t d", t=H), AX.X, AL.add), [e_p]))
            e_E = b.op("scalar", lambda e: e.activation(Et[:, :], SCt[:, :], AF.Exp), e_sc)
            e_dn = b.op("vector", lambda e: e.tensor_reduce(
                dent[:, :], Et[:, :].rearrange("p (h t) -> p h t", h=H), AX.X, AL.add), [e_E])
            e_rc = b.op("vector", lambda e: e.reciprocal(rect[:, :], dent[:, :]), [e_dn])
            for h in range(H):
                e_p2 = b.op("vector", lambda e, h=h, vsl=vsl: e.tensor_tensor(
                    prodC.rearrange("p (t d) -> p t d", t=H),
                    vsl.rearrange("p (t d) -> p t d", t=H),
                    Et[:, h * 16:(h + 1) * 16].unsqueeze(2).broadcast_to([128, H, 64]),
                    AL.mult), [e_E])
                e_cr = b.op("vector", lambda e: e.tensor_reduce(
                    ctxr[:, :], prodC.rearrange("p (t d) -> p d t", t=H), AX.X, AL.add), [e_p2])
                e_ct = b.op("vector", lambda e, tt=tt, h=h: e.tensor_scalar(
                    ctok[:, tt * D + h * 64: tt * D + (h + 1) * 64], ctxr[:, :],
                    rect[:, h:h + 1], None, AL.mult, AL.bypass), [e_cr, e_rc])
                ev_ctx_w[tt].append(e_ct)
                ev_qkv_read[par].append(e_ct)
        if bb == 0:
            tap_sb("ctx0", lambda: ctok[:, 0:D], [128, D], BF16, ev_ctx_w[0])

        ev_cw = []
        for tt in range(NT):
            ev_cw.append(b.dma(lambda e, bb=bb, tt=tt: e.dma_start(
                out=dt["s_ctx"][bb, tt * 128:(tt + 1) * 128, :],
                in_=ctok[:, tt * D:(tt + 1) * D]), ev_ctx_w[tt]))
        ctxq_lo = ternA
        ctxq_hi = ternB

        def ctxq_sl(hh):
            tb2 = ctxq_lo if hh < 8 else ctxq_hi
            return tb2[:, (hh % 8) * D:(hh % 8 + 1) * D]
        ev_cq = []
        for hh in range(H):
            ev_cq.append(b.dma(lambda e, bb=bb, hh=hh: e.dma_start(
                out=ctxq_sl(hh).rearrange("p (j d) -> p j d", j=16),
                in_=dt["s_ctx"][bb, :, hh * 64:(hh + 1) * 64].rearrange("(g j) d -> g j d", j=16)),
                ev_cw + [ev_a3_last]))
            ev_cq[-1] = fence()
        if bb == 0:
            tap_sb("ctxq0", lambda: ctxq_sl(0), [128, D], BF16, [ev_cq[0]])

        terno = ternC
        ev_tlo = []
        for r in range(8):
            ev_tlo.append(b.dma(lambda e, r=r: e.dma_start(
                out=terno[:, r * D:(r + 1) * D], in_=dt["s_terno"][r * 128:(r + 1) * 128, :]),
                [ev_to, ev_a3_last]))
        ev_tlof = fence()
        ev_cqi = []
        prev_ctxq_readers = []
        for tt in range(NT):
            e_q, e_a = emit_quant(ctxq_sl(tt), D, cqi[:, tt * D:(tt + 1) * D],
                                  [ev_cq[tt]], alpha_dst=alab[:, tt * 4:tt * 4 + 1])
            b.op("vector", lambda e, tt=tt: e.tensor_tensor(
                alab[:, tt * 4 + 1:tt * 4 + 2], alab[:, tt * 4:tt * 4 + 1],
                mwcols[:, 3:4], AL.mult), [e_a, ev_mwc])
            ev_cqi.append((e_q, b.ev("vector")))
            prev_ctxq_readers.append(e_q)
        ev_cqiT0 = b.dmat(lambda e: e.dma_start_transpose(
            cqiT[:, :].rearrange("p (m r) -> p m r", r=128), cqi[:, :]),
            [ev for ev, _ in ev_cqi] + [b.ev("tensor")])
        ev_cqiT = fence()
        ev_op_last = None
        prev_ew1 = None
        prev_ew2 = None
        prev_el1 = None
        for tt in range(NT):
            e_ev = []
            for et in range(2):
                pi, pt, fe = ps.acquire()
                mme = None
                for dc in range(8):
                    mme = b.op("tensor", lambda e, p=pt, dc=dc, tt=tt, et=et: e.matmul(
                        p[:, 0:512],
                        cqiT[:, (tt * 8 + dc) * 128:(tt * 8 + dc + 1) * 128],
                        terno[:, dc * D + et * 512: dc * D + (et + 1) * 512],
                        start=(dc == 0), stop=(dc == 7)),
                        [ev_cqiT, ev_tlof, fe] if dc == 0 else [])
                ev = b.op("scalar", lambda e, p=pt, tt=tt, et=et: e.activation(
                    aot[:, et * 512:(et + 1) * 512], p[:, 0:512],
                    AF.Identity, scale=alab[:, tt * 4 + 1:tt * 4 + 2]),
                    [mme, prev_ew1, prev_el1, ev_cqi[tt][1]])
                ps.release(pi, ev)
                e_ev.append(ev)
                ev_op_last = mme
            e_lno = emit_ln(aot, D, lnt, e_ev)
            evx0 = b.dma(lambda e, bb=bb, tt=tt: e.dma_start(
                out=x0t[:, :], in_=dt["x0"][bb, tt * 128:(tt + 1) * 128, :]),
                [b.ev("vector")])
            evx0 = fence()
            e_x1 = b.op("vector", lambda e: e.tensor_tensor(
                aot[:, 0:D], x0t[:, 0:D], lnt[:, 0:D], AL.add), [e_lno, evx0])
            ew1 = b.dma(lambda e, bb=bb, tt=tt: e.dma_start(
                out=dt["s_x1"][bb, tt * 128:(tt + 1) * 128, :], in_=aot[:, 0:D]), [e_x1])
            x1_done.append(ew1)
            prev_ew1 = ew1
            e_l1 = emit_ln(aot, D, lnt, [e_x1])
            prev_el1 = e_l1
            e_lb = b.op("vector", lambda e: e.tensor_copy(ln_bf[:, 0:D], lnt[:, 0:D]),
                        [e_l1, prev_ew2])
            qq = tt // 4
            ew2 = b.dma(lambda e, bb=bb, tt=tt, qq=qq: e.dma_start(
                out=dt["s_xln1"][qq, bb, (tt % 4) * 128:(tt % 4 + 1) * 128, :],
                in_=ln_bf[:, 0:D]), [e_lb])
            ew3 = b.dma(lambda e, bb=bb, tt=tt: e.dma_start(
                out=dt["a2a_in"][tt // 2, bb, (tt % 2) * 128:(tt % 2 + 1) * 128, :],
                in_=ln_bf[:, 0:D]), [e_lb])
            xln1_q_done[qq].append(ew2)
            xln1_q_done[qq].append(ew3)
            prev_ew2 = ew3
        prev_terno_mm = ev_op_last
        prev_a5_write = ev_cw
        prev_cqiT = ev_cqiT

    tap_dram("x1", dt["s_x1"], [BPC, S, D], F32, x1_done)
    tap_dram("xln1", dt["s_xln1"], [4, BPC, 512, D], BF16,
             [ev for q in xln1_q_done for ev in q])

    A.reset()
    b.barrier()

    if _PHASES >= 2:
        ev_a2a = b.gp(lambda e: e.collective_compute(
            "AllToAll", mybir.AluOpType.bypass,
            replica_groups=[list(range(NCORE))],
            ins=[dt["a2a_in"].opt()],
            outs=[dt["a2a_out"].opt()]),
            [ev for q in range(4) for ev in xln1_q_done[q]])
        tap_dram("a2ao", dt["a2a_out"], [NCORE, BPC, 256, D], BF16, [ev_a2a])

        A.mark()
        wgb = {}
        ev_wg = {}
        for wname in ("wfT", "wiT", "wcT", "ufT", "uiT", "ucT"):
            wgb[wname] = A.alloc(BF16, 8 * D)
            evs = []
            for r in range(8):
                evs.append(b.dma(lambda e, r=r, wname=wname: e.dma_start(
                    out=wgb[wname][:, r * D:(r + 1) * D],
                    in_=dt["s_" + wname][r * 128:(r + 1) * 128, :]), []))
            evf = fence()
            ev_wg[wname] = [evf] * 8
        xf_tok = [A.alloc(BF16, 2 * D), A.alloc(BF16, 2 * D)]
        xfT = [A.alloc(BF16, 2048), A.alloc(BF16, 2048)]
        c_st = A.alloc(F32, 8 * 256)
        cbf = A.alloc(BF16, 8 * 256)
        g_f = big1
        g_i = big2
        g_g = big3
        t1x = A.alloc(F32, 256)
        t2x = A.alloc(F32, 256)
        e_ci = b.op("vector", lambda e: e.memset(c_st[:, :], 0.0), [b.ev("sync")])
        for mc in range(8):
            e_ci = b.op("vector", lambda e, mc=mc: e.tensor_scalar(
                c_st[:, mc * 256:(mc + 1) * 256], c_st[:, mc * 256:(mc + 1) * 256],
                incc_s[:, mc:mc + 1], None, AL.add, AL.bypass), [e_ci, evc[5]])
        ev_cbf = b.op("vector", lambda e: e.tensor_copy(cbf[:, :], c_st[:, :]), [e_ci])

        ev_xf_use = [None, None]
        for t in range(16):
            srcc, j = t // 2, t % 2
            par = t % 2
            ev_ld = b.dma(lambda e, srcc=srcc, j=j, par=par: e.dma_start(
                out=xf_tok[par][:, :].rearrange("p (a d) -> p a d", a=2),
                in_=dt["a2a_out"][srcc, j].rearrange("(a p) d -> p a d", a=2)),
                [ev_a2a, ev_xf_use[par]])
            ev_tr = []
            for j2 in range(2):
                ev_tr.append(b.dmat(lambda e, par=par, j2=j2: e.dma_start_transpose(
                    xfT[par][:, j2 * 128:].rearrange("p (dc r) -> p dc r", r=128)
                    if False else
                    xfT[par][:, :].rearrange("p (dc j2r r) -> p dc j2r r", dc=8, j2r=2, r=128)[:, :, j2, :],
                    xf_tok[par][:, j2 * D:(j2 + 1) * D]), [ev_ld]))
            ev_tr = [fence()]
            ev_ev = {}
            for gi_, (wname, uname, gbuf, biast, actf) in enumerate((
                    ("wfT", "ufT", g_f, biasf, AF.Sigmoid),
                    ("wiT", "uiT", g_i, biasi, AF.Sigmoid),
                    ("wcT", "ucT", g_g, biasc, AF.Tanh))):
                for fc in range(8):
                    pi, pt, fe = ps.acquire()
                    mme = None
                    for dc in range(8):
                        mme = b.op("tensor", lambda e, p=pt, dc=dc, fc=fc, wname=wname, par=par: e.matmul(
                            p[:, 0:256],
                            wgb[wname][:, dc * D + fc * 128: dc * D + (fc + 1) * 128],
                            xfT[par][:, dc * 256:(dc + 1) * 256],
                            start=(dc == 0), stop=False),
                            [ev_wg[wname][dc], ev_tr[0], fe] if dc == 0
                            else [ev_wg[wname][dc]])
                    for mc in range(8):
                        mme = b.op("tensor", lambda e, p=pt, mc=mc, fc=fc, uname=uname: e.matmul(
                            p[:, 0:256],
                            wgb[uname][:, mc * D + fc * 128: mc * D + (fc + 1) * 128],
                            cbf[:, mc * 256:(mc + 1) * 256],
                            start=False, stop=(mc == 7)),
                            [ev_wg[uname][mc], ev_cbf] if mc == 0 else [ev_wg[uname][mc]])
                    ev = b.op("scalar", lambda e, p=pt, gbuf=gbuf, fc=fc, biast=biast, actf=actf: e.activation(
                        gbuf[:, fc * 256:(fc + 1) * 256], p[:, 0:256], actf,
                        bias=biast[:, fc:fc + 1]), [mme])
                    ps.release(pi, ev)
                    ev_ev[(gi_, fc)] = ev
            up_evs = []
            for fc in range(8):
                sl = slice(fc * 256, (fc + 1) * 256)
                e_u1 = b.op("vector", lambda e, sl=sl: e.tensor_tensor(
                    t1x[:, 0:256], g_i[:, sl], g_g[:, sl], AL.mult),
                    [ev_ev[(1, fc)], ev_ev[(2, fc)]])
                e_u2 = b.op("vector", lambda e, sl=sl: e.tensor_tensor(
                    t2x[:, 0:256], g_f[:, sl], c_st[:, sl], AL.mult),
                    [ev_ev[(0, fc)], e_u1])
                e_u3 = b.op("vector", lambda e, sl=sl: e.tensor_tensor(
                    c_st[:, sl], t2x[:, 0:256], t1x[:, 0:256], AL.add), [e_u2])
                up_evs.append(b.op("vector", lambda e, sl=sl: e.tensor_copy(
                    cbf[:, sl], c_st[:, sl]), [e_u3, ev_ev[(2, 7)]]))
            ev_cbf = up_evs[-1]
            ev_xf_use[par] = ev_ev[(2, 7)]
        cf_evs = []
        for mc in range(8):
            cf_evs.append(b.dma(lambda e, mc=mc: e.dma_start(
                out=dt["s_cfin"][mc * 128:(mc + 1) * 128, :],
                in_=cbf[:, mc * 256:(mc + 1) * 256]), [ev_cbf]))
        ev_agcf = b.gp(lambda e: e.collective_compute(
            "AllGather", mybir.AluOpType.bypass,
            replica_groups=[list(range(NCORE))],
            ins=[dt["s_cfin"].opt()], outs=[dt["ag_cf"].opt()]), cf_evs)
        if "cfin" in dbg_names:
            tap_dram("cfin", dt["ag_cf"], [NCORE, D, 256], BF16, [ev_agcf])
        A.reset()
        b.barrier()

        A.mark()
        cu = A.alloc(F32, NT * D)
        uob = A.alloc(BF16, 8 * D)
        wo2b = A.alloc(BF16, 8 * D)
        cfr = [A.alloc(BF16, 2048), A.alloc(BF16, 2048)]
        xlq = [A.alloc(BF16, 4096), A.alloc(BF16, 4096)]
        ev_uo = []
        ev_wo2 = []
        for r in range(8):
            ev_uo.append(b.dma(lambda e, r=r: e.dma_start(
                out=uob[:, r * D:(r + 1) * D], in_=dt["s_uoT"][r * 128:(r + 1) * 128, :]), []))
            ev_wo2.append(b.dma(lambda e, r=r: e.dma_start(
                out=wo2b[:, r * D:(r + 1) * D], in_=dt["s_wo2T"][r * 128:(r + 1) * 128, :]), []))
        ev_uwf = fence()
        ev_cu = {}
        ev_cf_use = [None, None]
        for r8 in range(8):
            par = r8 % 2
            ev_cfl0 = b.dma(lambda e, r8=r8, par=par: e.dma_start(
                out=cfr[par][:, :].rearrange("p (mc s) -> p mc s", mc=8),
                in_=dt["ag_cf"][r8].rearrange("(mc p) s -> p mc s", p=128)),
                [ev_agcf, ev_cf_use[par]])
            ev_cfl = fence()
            for sc in range(2):
                tt = r8 * 2 + sc
                for et in range(2):
                    pi, pt, fe = ps.acquire()
                    mme = None
                    for mc in range(8):
                        mme = b.op("tensor", lambda e, p=pt, mc=mc, sc=sc, et=et, par=par: e.matmul(
                            p[:, 0:512],
                            cfr[par][:, mc * 256 + sc * 128: mc * 256 + (sc + 1) * 128],
                            uob[:, mc * D + et * 512: mc * D + (et + 1) * 512],
                            start=(mc == 0), stop=False),
                            [ev_cfl, ev_uwf, fe] if mc == 0 else [])
                    mme = b.op("tensor", lambda e, p=pt, et=et: e.matmul(
                        p[:, 0:512], onesr_b[:, :], bo_b[:, et * 512:(et + 1) * 512],
                        start=False, stop=True), [ev_onb, ev_bob])
                    ev = b.op("vector", lambda e, p=pt, tt=tt, et=et: e.tensor_copy(
                        cu[:, tt * D + et * 512: tt * D + (et + 1) * 512], p[:, 0:512]), [mme])
                    ps.release(pi, ev)
                    ev_cu[(tt, et)] = ev
            ev_cf_use[par] = ev_cu[(r8 * 2 + 1, 1)]
        x2_done = []
        prev_x2w = None
        prev_xl2w = None
        ev_xlq_use = [None, None]
        for bb in range(BPC):
            for qq in range(4):
                par = qq % 2
                ev_xl0 = b.dmat(lambda e, bb=bb, qq=qq, par=par: e.dma_start_transpose(
                    xlq[par][:, :].rearrange("p (dc s) -> p dc s", dc=8),
                    dt["s_xln1"][qq, bb]), [xln1_q_done[qq][-1], ev_cf_use[0], ev_cf_use[1],
                                            ev_xlq_use[par]])
                ev_xl = fence()
                for ts4 in range(4):
                    tt = qq * 4 + ts4
                    e_hev = []
                    for et in range(2):
                        pi, pt, fe = ps.acquire()
                        mme = None
                        for dc in range(8):
                            mme = b.op("tensor", lambda e, p=pt, dc=dc, ts4=ts4, et=et, par=par: e.matmul(
                                p[:, 0:512],
                                xlq[par][:, dc * 512 + ts4 * 128: dc * 512 + (ts4 + 1) * 128],
                                wo2b[:, dc * D + et * 512: dc * D + (et + 1) * 512],
                                start=(dc == 0), stop=(dc == 7)),
                                [ev_xl, ev_uwf, fe] if dc == 0 else [])
                        e_ad = b.op("vector", lambda e, p=pt, tt=tt, et=et: e.tensor_tensor(
                            d_t[:, et * 512:(et + 1) * 512], p[:, 0:512],
                            cu[:, tt * D + et * 512: tt * D + (et + 1) * 512], AL.add),
                            [mme, ev_cu[(tt, et)], prev_x2w])
                        ev = b.op("scalar", lambda e, et=et: e.activation(
                            xln[:, et * 512:(et + 1) * 512], d_t[:, et * 512:(et + 1) * 512],
                            AF.Tanh), [e_ad])
                        ps.release(pi, b.ev("scalar"))
                        e_hev.append(ev)
                        if ts4 == 3:
                            ev_xlq_use[par] = ev
                    e_lnh = emit_ln(xln, D, lnt, e_hev)
                    evx1 = b.dma(lambda e, bb=bb, tt=tt: e.dma_start(
                        out=x0t[:, :], in_=dt["s_x1"][bb, tt * 128:(tt + 1) * 128, :]),
                        [b.ev("vector")] + x1_done)
                    evx1 = fence()
                    e_x2 = b.op("vector", lambda e: e.tensor_tensor(
                        d_t[:, 0:D], x0t[:, 0:D], lnt[:, 0:D], AL.add),
                        [e_lnh, evx1, prev_x2w])
                    ew1 = b.dma(lambda e, bb=bb, tt=tt: e.dma_start(
                        out=dt["s_x2"][bb, tt * 128:(tt + 1) * 128, :], in_=d_t[:, 0:D]), [e_x2])
                    x2_done.append(ew1)
                    prev_x2w = ew1
                    e_l2 = emit_ln(d_t, D, lnt, [e_x2])
                    e_lb = b.op("vector", lambda e: e.tensor_copy(ln_bf[:, 0:D], lnt[:, 0:D]),
                                [e_l2, prev_xl2w])
                    ew2 = b.dma(lambda e, bb=bb, tt=tt: e.dma_start(
                        out=dt["s_xln2"][bb, tt * 128:(tt + 1) * 128, :], in_=ln_bf[:, 0:D]), [e_lb])
                    x2_done.append(ew2)
                    prev_xl2w = ew2
        tap_dram("x2", dt["s_x2"], [BPC, S, D], F32, x2_done)
        tap_dram("xln2", dt["s_xln2"], [BPC, S, D], BF16, x2_done)
        A.reset()
        b.barrier()

    if _PHASES >= 3:
        A.mark()
        identf2 = A.alloc(F32, 128)
        sltA_f = A.alloc(F32, 128)
        triuI_f = A.alloc(F32, 128)
        sel127f = A.alloc(F32, 128)
        dbc = A.alloc(F32, 32)
        dibuf = A.alloc(BF16, NHEADS * 128)
        e_c1 = b.dma(lambda e: e.dma_start(out=identf2[:, :], in_=dt["identw"][:, :]))
        e_c2 = b.dma(lambda e: e.dma_start(out=sltA_f[:, :], in_=dt["sltA"][:, :]))
        e_c3 = b.dma(lambda e: e.dma_start(out=triuI_f[:, :], in_=dt["triuI"][:, :]))
        e_c4 = b.dma(lambda e: e.dma_start(out=sel127f[:, :], in_=dt["sel127"][:, :]))
        e_c1 = e_c2 = e_c3 = e_c4 = fence()
        sltA_b = A.alloc(BF16, 128)
        e_c5 = b.op("vector", lambda e: e.tensor_copy(sltA_b[:, :], sltA_f[:, :]), [e_c2])
        pi, pt, fe = ps.acquire()
        e_db = b.op("tensor", lambda e, p=pt: e.matmul(
            p[:, 0:32], onesr_f[:, :], dvec_s[:, :], start=True, stop=True),
            [evc[0], evc[8], fe])
        e_db2 = b.op("vector", lambda e, p=pt: e.tensor_copy(dbc[:, :], p[:, 0:32]), [e_db])
        ps.release(pi, e_db2)
        ev_dib = []
        for h in range(NHEADS):
            ev_dib.append(b.op("vector", lambda e, h=h: e.tensor_scalar(
                dibuf[:, h * 128:(h + 1) * 128], identf2[:, :], dbc[:, h:h + 1], None,
                AL.mult, AL.bypass), [e_db2, e_c1]))

        x3_done = []
        prev_x3w = None
        prev_xl3w = None
        for bb in range(BPC):
            A.mark()
            Bfeat = A.alloc(BF16, S)
            Cfeat = A.alloc(BF16, S)
            dtraw = A.alloc(F32, S)
            A.mark()
            xlT = A.alloc(BF16, 8 * S)
            winb = [A.alloc(BF16, 8 * 512), A.alloc(BF16, 8 * 512)]
            winx = [A.alloc(BF16, 8 * 128), A.alloc(BF16, 8 * 128)]
            ztile = A.alloc(BF16, DI)
            xbcp = [A.alloc(BF16, 2052), A.alloc(BF16, 2052), A.alloc(BF16, 2052)]
            xsout = [A.alloc(BF16, S), A.alloc(BF16, S)]
            cvt = big1
            cvt2 = big2
            ev_xlT = b.dmat(lambda e, bb=bb: e.dma_start_transpose(
                xlT[:, :].rearrange("p (dc s) -> p dc s", dc=8), dt["s_xln2"][bb]),
                [x2_done[-1] if _PHASES >= 2 else None, b.ev("tensor")])
            ev_wu = [None, None]
            for et in range(4):
                par = et % 2
                ev_wl = []
                for r in range(8):
                    ev_wl.append(b.dma(lambda e, r=r, et=et, par=par: e.dma_start(
                        out=winb[par][:, r * 512:(r + 1) * 512],
                        in_=dt["s_win"][r * 128:(r + 1) * 128, et * 512:(et + 1) * 512]),
                        [ev_wu[par]]))
                ev_wf = fence()
                for tt in range(NT):
                    pi, pt, fe = ps.acquire()
                    mme = None
                    for dc in range(8):
                        mme = b.op("tensor", lambda e, p=pt, dc=dc, tt=tt, par=par: e.matmul(
                            p[:, 0:512],
                            xlT[:, dc * S + tt * 128: dc * S + (tt + 1) * 128],
                            winb[par][:, dc * 512:(dc + 1) * 512],
                            start=(dc == 0), stop=(dc == 7)),
                            [ev_xlT, ev_wf, fe] if dc == 0 else [])
                    ev = b.op("scalar", lambda e, p=pt: e.activation(
                        ztile[:, 0:512], p[:, 0:512], AF.Silu), [mme, b.ev("sync")])
                    evz = b.dma(lambda e, bb=bb, tt=tt, et=et: e.dma_start(
                        out=dt["s_z"][bb, tt * 128:(tt + 1) * 128, et * 512:(et + 1) * 512],
                        in_=ztile[:, 0:512]), [ev])
                    ps.release(pi, ev)
                    if tt == NT - 1:
                        ev_wu[par] = mme
            ev_xs = []
            ev_xsd = []
            ev_wxu = [None, None]
            prev_sil = {}
            prev_sil_g = None
            for ec in range(19):
                par = ec % 2
                nr = 128 if ec < 18 else 32
                ev_wl = []
                for r in range(8):
                    ev_wl.append(b.dma(lambda e, r=r, ec=ec, par=par, nr=nr: e.dma_start(
                        out=winx[par][:, r * 128:r * 128 + nr],
                        in_=dt["s_win"][r * 128:(r + 1) * 128, 2048 + ec * 128: 2048 + ec * 128 + nr]),
                        [ev_wxu[par]]))
                ev_wxf = fence()
                xb = xbcp[ec % 3]
                e_pad = b.op("vector", lambda e, xb=xb: e.memset(xb[:, 0:4], 0.0),
                             [prev_sil.get(ec % 3)])
                mml = None
                for ts in range(4):
                    pi, pt, fe = ps.acquire()
                    mme = None
                    for dc in range(8):
                        mme = b.op("tensor", lambda e, p=pt, dc=dc, ts=ts, par=par, nr=nr: e.matmul(
                            p[0:nr, 0:512],
                            winx[par][:, dc * 128:dc * 128 + nr],
                            xlT[:, dc * S + ts * 512:(dc) * S + (ts + 1) * 512],
                            start=(dc == 0), stop=(dc == 7)),
                            [ev_wxf, fe] if dc == 0 else [])
                    if ec < 18:
                        ev = b.op("vector", lambda e, p=pt, xb=xb, ts=ts, nr=nr: e.tensor_copy(
                            xb[:, 4 + ts * 512: 4 + (ts + 1) * 512], p[0:nr, 0:512]),
                            [mme, e_pad])
                    else:
                        ev = b.op("vector", lambda e, p=pt, ts=ts: e.tensor_copy(
                            dtraw[0:32, ts * 512:(ts + 1) * 512], p[0:32, 0:512]), [mme])
                    ps.release(pi, ev)
                    mml = ev
                ev_wxu[par] = mml
                if ec < 18:
                    e_a1 = b.op("vector", lambda e, xb=xb, ec=ec: e.tensor_scalar(
                        cvt[:, 0:S], xb[:, 1:1 + S], convw_s[:, ec * 4:ec * 4 + 1], None,
                        AL.mult, AL.bypass), [mml, evc[10]])
                    e_a2 = b.op("vector", lambda e, xb=xb, ec=ec: e.scalar_tensor_tensor(
                        cvt2[:, 0:S], xb[:, 2:2 + S], convw_s[:, ec * 4 + 1:ec * 4 + 2],
                        cvt[:, 0:S], AL.mult, AL.add), [e_a1, prev_sil_g])
                    e_a3 = b.op("vector", lambda e, xb=xb, ec=ec: e.scalar_tensor_tensor(
                        cvt[:, 0:S], xb[:, 3:3 + S], convw_s[:, ec * 4 + 2:ec * 4 + 3],
                        cvt2[:, 0:S], AL.mult, AL.add), [e_a2])
                    e_a4 = b.op("vector", lambda e, xb=xb, ec=ec: e.scalar_tensor_tensor(
                        cvt2[:, 0:S], xb[:, 4:4 + S], convw_s[:, ec * 4 + 3:ec * 4 + 4],
                        cvt[:, 0:S], AL.mult, AL.add), [e_a3])
                    if ec < 16:
                        dst = xsout[ec % 2][:, 0:S]
                    elif ec == 16:
                        dst = Bfeat[:, :]
                    else:
                        dst = Cfeat[:, :]
                    e_sil = b.op("scalar", lambda e, dst=dst, ec=ec: e.activation(
                        dst, cvt2[:, 0:S], AF.Silu, bias=convb_s[:, ec:ec + 1]),
                        [e_a4, evc[11]] + ([ev_xsd[ec - 2]] if ec >= 2 and ec - 2 < 16 else []))
                    prev_sil[ec % 3] = e_sil
                    prev_sil_g = e_sil
                    if ec < 16:
                        ev_xsd.append(b.dma(lambda e, bb=bb, ec=ec: e.dma_start(
                            out=dt["s_xs"][bb, ec * 128:(ec + 1) * 128, :],
                            in_=xsout[ec % 2][:, 0:S]), [e_sil]))
                    ev_xs.append(e_sil)
            if bb == 0:
                tap_sb("Bf0", lambda: Bfeat[:, :], [128, S], BF16, [ev_xs[16]])
                tap_sb("Cf0", lambda: Cfeat[:, :], [128, S], BF16, [ev_xs[17]])
            A.reset()
            b.barrier()

            A.mark()
            dt_sp = A.alloc(F32, S)
            lam = A.alloc(F32, S)
            Lc_ = A.alloc(F32, S)
            S_st = A.alloc(F32, 2048)
            x_tok = [A.alloc(BF16, 2048), A.alloc(BF16, 2048)]
            xw_tok = A.alloc(BF16, 2048)
            y_tile = A.alloc(BF16, 2048)
            z_c = A.alloc(BF16, 2048)
            B_tok = A.alloc(BF16, 2048)
            LT = A.alloc(F32, 32)
            negLT = A.alloc(F32, 32)
            expLT = A.alloc(F32, 32)
            lamT = A.alloc(F32, 32)
            wdtT = A.alloc(F32, 32)
            dtT = A.alloc(F32, 32)
            w32 = A.alloc(F32, 128)
            wdt32 = A.alloc(F32, 128)
            decD = A.alloc(F32, 32)
            bband = A.alloc(BF16, 512)
            E_h = A.alloc(BF16, 128)
            G_h = A.alloc(BF16, 128)
            G3_h = A.alloc(BF16, 128)
            y2s = A.alloc(F32, 512)
            e_d1 = b.op("scalar", lambda e: e.activation(
                dt_sp[0:32, 0:S], dtraw[0:32, 0:S], AF.Exp, bias=dtb_s[0:32, 0:1]),
                [b.ev("vector"), evc[7]])
            e_d2 = b.op("scalar", lambda e: e.activation(
                dt_sp[0:32, 0:S], dt_sp[0:32, 0:S], AF.Ln, bias=1.0), [e_d1])
            e_d3 = b.op("vector", lambda e: e.tensor_scalar(
                lam[0:32, 0:S], dt_sp[0:32, 0:S], nega_s[0:32, 0:1], None,
                AL.mult, AL.bypass), [e_d2, ev_na])
            ev_scan = []
            for cch in range(NCHUNK):
                ev_scan.append(b.op("vector", lambda e, cch=cch: e.tensor_tensor_scan(
                    Lc_[0:32, cch * 128:(cch + 1) * 128],
                    lam[0:32, cch * 128:(cch + 1) * 128],
                    zero32[0:32, 0:128], 0.0, AL.add, AL.add), [e_d3, ev_z32]))
            if bb == 0:
                tap_sb("dts", lambda: dt_sp[0:32, 0:S], [32, S], F32, [e_d2])
                tap_sb("lam0", lambda: lam[0:32, 0:S], [32, S], F32, [e_d3])
                tap_sb("Lc0", lambda: Lc_[0:32, 0:S], [32, S], F32, list(ev_scan))
            e_S0 = b.op("vector", lambda e: e.memset(S_st[:, :], 0.0), [b.ev("tensor")])
            ev_Supd = e_S0
            ev_bt0 = b.dmat(lambda e: e.dma_start_transpose(
                B_tok[:, :].rearrange("p (m r) -> p m r", r=128), Bfeat[:, :]),
                [ev_xs[16] if len(ev_xs) > 16 else b.ev("scalar"), b.ev("tensor")])
            ev_bt = fence()
            ev_ynw_prev = None
            ev_xtok_use = [None, None]
            S_bf = A.alloc(BF16, 2048)
            CB_s = A.alloc(F32, 128)
            ev_cbs_use = None
            Supd_t = A.alloc(F32, 512)
            ident4b = A.alloc(BF16, 512)
            i4evs = []
            for k4 in range(4):
                i4evs.append(b.op("vector", lambda e, k4=k4: e.tensor_copy(
                    ident4b[:, k4 * 128:(k4 + 1) * 128], ident_b[:, :]), [ev_idb]))
            e_Sb0 = b.op("vector", lambda e: e.memset(S_bf[:, :], 0.0), [b.ev("tensor")])
            ev_Sbf = [e_Sb0] * 4
            ev_Sst = [e_S0] * 4
            ev_bb_use = None
            ev_zc_use = None
            prev_ynw = None
            for cch in range(NCHUNK):
                par = cch % 2
                tevs = []
                levs = []
                for jj in range(4):
                    tevs.append(b.op("vector", lambda e, cch=cch, jj=jj: e.transpose(
                        LT[jj * 32:(jj + 1) * 32, 0:32],
                        Lc_[0:32, cch * 128 + jj * 32: cch * 128 + (jj + 1) * 32]),
                        [ev_scan[cch], b.ev("scalar")]))
                    levs.append(b.op("vector", lambda e, cch=cch, jj=jj: e.transpose(
                        lamT[jj * 32:(jj + 1) * 32, 0:32],
                        lam[0:32, cch * 128 + jj * 32: cch * 128 + (jj + 1) * 32]),
                        [e_d3]))
                e_nL = b.op("vector", lambda e: e.tensor_scalar(
                    negLT[:, 0:32], LT[:, 0:32], -1.0, None, AL.mult, AL.bypass),
                    tevs + [b.ev("scalar")])
                e_eL0 = b.op("scalar", lambda e: e.activation(
                    expLT[:, 0:32], LT[:, 0:32], AF.Exp), tevs + [b.ev("vector"), b.ev("tensor")])
                e_w1 = b.op("vector", lambda e, cch=cch: e.tensor_scalar(
                    w32[0:32, 0:128], Lc_[0:32, cch * 128:(cch + 1) * 128],
                    Lc_[0:32, cch * 128 + 127: cch * 128 + 128], None,
                    AL.subtract, AL.bypass), [ev_scan[cch], b.ev("scalar")])
                e_w2 = b.op("scalar", lambda e: e.activation(
                    w32[0:32, 0:128], w32[0:32, 0:128], AF.Exp, scale=-1.0), [e_w1])
                e_w3 = b.op("vector", lambda e, cch=cch: e.tensor_tensor(
                    wdt32[0:32, 0:128], w32[0:32, 0:128],
                    dt_sp[0:32, cch * 128:(cch + 1) * 128], AL.mult), [e_w2, e_d2])
                wevs = []
                devs = []
                for jj in range(4):
                    wevs.append(b.op("vector", lambda e, jj=jj: e.transpose(
                        wdtT[jj * 32:(jj + 1) * 32, 0:32],
                        wdt32[0:32, jj * 32:(jj + 1) * 32]), [e_w3]))
                    devs.append(b.op("vector", lambda e, cch=cch, jj=jj: e.transpose(
                        dtT[jj * 32:(jj + 1) * 32, 0:32],
                        dt_sp[0:32, cch * 128 + jj * 32: cch * 128 + (jj + 1) * 32]), [e_d2]))
                e_xt0 = b.dmat(lambda e, bb=bb, cch=cch, par=par: e.dma_start_transpose(
                    x_tok[par][:, :].rearrange("p (m r) -> p m r", m=1),
                    dt["s_xs"][bb, :, cch * 128:(cch + 1) * 128]),
                    ev_xsd[0:16] + [ev_xtok_use[par]])
                e_xt = fence()
                e_xw = b.op("vector", lambda e, par=par: e.tensor_tensor(
                    xw_tok[:, :].rearrange("p (h d) -> p h d", h=NHEADS),
                    x_tok[par][:, :].rearrange("p (h d) -> p h d", h=NHEADS),
                    wdtT[:, 0:32].unsqueeze(2).broadcast_to([128, 32, PHD]), AL.mult),
                    [e_xt] + wevs + [b.ev("tensor")])
                pcb, ptcb, fecb = ps.acquire()
                e_cb0 = b.op("tensor", lambda e, p=ptcb, cch=cch: e.matmul(
                    p[:, 0:128], Bfeat[:, cch * 128:(cch + 1) * 128],
                    Cfeat[:, cch * 128:(cch + 1) * 128], start=True, stop=True),
                    [ev_xs[16], ev_xs[17], fecb])
                e_cb = b.op("vector", lambda e, p=ptcb: e.tensor_copy(
                    CB_s[:, 0:128], p[:, 0:128]), [e_cb0, ev_cbs_use])
                ps.release(pcb, e_cb)
                pdc, ptdc, fedc = ps.acquire()
                e_dc = b.op("tensor", lambda e, p=ptdc: e.matmul(
                    p[:, 0:32], sel127f[:, :], expLT[:, 0:32], start=True, stop=True),
                    [e_eL0, e_c4, fedc])
                e_dc2 = b.op("vector", lambda e, p=ptdc: e.tensor_copy(decD[:, 0:32], p[:, 0:32]), [e_dc])
                ps.release(pdc, e_dc2)
                last_G = None
                for hg in range(4):
                    pY1, ptY1, feY1 = ps.acquire()
                    pY2, ptY2, feY2 = ps.acquire()
                    pT, ptT, feT = ps.acquire()
                    ev_y1mm = []
                    ev_y2mm = []
                    ev_tmm = []
                    ptE = None
                    pE = None
                    last_exp = None
                    mE2 = None
                    for hh in range(8):
                        h = hg * 8 + hh
                        if hh % 4 == 0:
                            if ptE is not None:
                                ps.release(pE, last_exp)
                            pE, ptE, feE = ps.acquire()
                            bevs = []
                            for k4 in range(4):
                                bevs.append(b.op("vector", lambda e, h=h, k4=k4: e.tensor_scalar(
                                    bband[:, k4 * 128:(k4 + 1) * 128], triuI_f[:, :],
                                    lamT[:, h + k4:h + k4 + 1], None, AL.mult, AL.bypass),
                                    levs + [e_c3, ev_bb_use]))
                            mE = b.op("tensor", lambda e, p=ptE: e.matmul(
                                p[:, 0:512], sltA_b[:, :], bband[:, 0:512],
                                start=True, stop=False), bevs + [e_c5, feE])
                            mE2 = b.op("tensor", lambda e, p=ptE: e.matmul(
                                p[:, 0:512], slneg_b[:, :], ident4b[:, 0:512],
                                start=False, stop=True), i4evs + [ev_slb])
                            ev_bb_use = mE
                        last_exp = b.op("scalar", lambda e, p=ptE, hh=hh, h=h: e.activation(
                            E_h[:, 0:128], p[:, (hh % 4) * 128:(hh % 4 + 1) * 128],
                            AF.Exp), [mE2, e_nL, b.ev("vector")])
                        if bb == 0 and cch == 0 and h == 0:
                            tap_sb("E00", lambda: E_h[:, 0:128], [128, 128], BF16, [last_exp])
                        if bb == 0 and cch == 0 and h == 1:
                            tap_sb("E01", lambda: E_h[:, 0:128], [128, 128], BF16, [last_exp])
                        if bb == 0 and cch == 0 and h == 20:
                            tap_sb("E20", lambda: E_h[:, 0:128], [128, 128], BF16, [last_exp])
                        if bb == 0 and cch == 0 and h == 26:
                            tap_sb("E26", lambda: E_h[:, 0:128], [128, 128], BF16, [last_exp])
                        e_G = b.op("vector", lambda e, h=h: e.scalar_tensor_tensor(
                            G_h[:, 0:128], E_h[:, 0:128], dtT[:, h:h + 1], CB_s[:, 0:128],
                            AL.mult, AL.mult), [last_exp, e_cb] + devs)
                        e_G3 = b.op("vector", lambda e, h=h: e.tensor_tensor(
                            G3_h[:, 0:128], G_h[:, 0:128], dibuf[:, h * 128:(h + 1) * 128],
                            AL.add), [e_G, ev_dib[h], b.ev("tensor")])
                        if bb == 0 and cch == 0 and h == 20:
                            tap_sb("G320", lambda: G3_h[:, 0:128], [128, 128], BF16, [e_G3])
                        last_G = e_G
                        ev_y1mm.append(b.op("tensor", lambda e, p=ptY1, hh=hh, h=h, par=par: e.matmul(
                            p[:, hh * 64:(hh + 1) * 64], G3_h[:, 0:128],
                            x_tok[par][:, h * 64:(h + 1) * 64], start=True, stop=True),
                            [e_G3, e_xt] + ([feY1] if hh == 0 else [])))
                        ev_y2mm.append(b.op("tensor", lambda e, p=ptY2, hh=hh, h=h, cch=cch: e.matmul(
                            p[:, hh * 64:(hh + 1) * 64], Cfeat[:, cch * 128:(cch + 1) * 128],
                            S_bf[:, h * 64:(h + 1) * 64], start=True, stop=True),
                            [ev_Sbf[hg]] + ([feY2] if hh == 0 else [])))
                        ev_tmm.append(b.op("tensor", lambda e, p=ptT, hh=hh, h=h, cch=cch: e.matmul(
                            p[:, hh * 64:(hh + 1) * 64], B_tok[:, cch * 128:(cch + 1) * 128],
                            xw_tok[:, h * 64:(h + 1) * 64], start=True, stop=True),
                            [e_xw, ev_bt] + ([feT] if hh == 0 else [])))
                    ps.release(pE, last_exp)
                    hgsl = slice(hg * 512, (hg + 1) * 512)
                    e_y2s = b.op("vector", lambda e, p=ptY2, hg=hg: e.tensor_tensor(
                        y2s[:, 0:512].rearrange("p (h d) -> p h d", h=8),
                        p[:, 0:512].rearrange("p (h d) -> p h d", h=8),
                        expLT[:, hg * 8:(hg + 1) * 8].unsqueeze(2).broadcast_to([128, 8, PHD]),
                        AL.mult), ev_y2mm + [e_eL0])
                    e_y = b.op("vector", lambda e, p=ptY1, hg=hg: e.tensor_tensor(
                        y_tile[:, hg * 512:(hg + 1) * 512], y2s[:, 0:512], p[:, 0:512], AL.add),
                        [e_y2s] + ev_y1mm + [ev_ynw_prev])
                    ps.release(pY1, e_y)
                    ps.release(pY2, e_y2s)
                    e_su1 = b.op("vector", lambda e, hg=hg: e.tensor_tensor(
                        Supd_t[:, 0:512].rearrange("p (h d) -> p h d", h=8),
                        S_st[:, hg * 512:(hg + 1) * 512].rearrange("p (h d) -> p h d", h=8),
                        decD[:, hg * 8:(hg + 1) * 8].unsqueeze(2).broadcast_to([128, 8, PHD]),
                        AL.mult), [e_dc2, ev_Sst[hg]])
                    e_su2 = b.op("vector", lambda e, p=ptT, hg=hg: e.tensor_tensor(
                        S_st[:, hg * 512:(hg + 1) * 512], Supd_t[:, 0:512], p[:, 0:512], AL.add),
                        [e_su1] + ev_tmm)
                    ps.release(pT, e_su2)
                    ev_Sst[hg] = e_su2
                    ev_Sbf[hg] = b.op("vector", lambda e, hg=hg: e.tensor_copy(
                        S_bf[:, hg * 512:(hg + 1) * 512], S_st[:, hg * 512:(hg + 1) * 512]),
                        [e_su2] + ev_y2mm)
                ev_cbs_use = last_G
                if bb == 0 and cch == 0:
                    tap_sb("y0t", lambda: y_tile[:, :], [128, 2048], BF16,
                           [ev_Sbf[0], ev_Sbf[1], ev_Sbf[2], ev_Sbf[3]])
                e_zl0 = b.dma(lambda e, bb=bb, cch=cch: e.dma_start(
                    out=z_c[:, :], in_=dt["s_z"][bb, cch * 128:(cch + 1) * 128, :]),
                    [ev_zc_use])
                e_zl = fence()
                e_yg = b.op("vector", lambda e: e.tensor_tensor(
                    y_tile[:, :], y_tile[:, :], z_c[:, :], AL.mult),
                    [e_zl, b.ev("vector")])
                ev_zc_use = e_yg
                e_sq2 = b.op("scalar", lambda e: e.activation(
                    junk[:, 0:2048], y_tile[:, :], AF.Square, accum_out=stats[:, 19:20]),
                    [e_yg, b.ev("vector")])
                e_ms2 = b.op("vector", lambda e: e.tensor_scalar(
                    stats[:, 20:21], stats[:, 19:20], 1.0 / DI, 1e-5, AL.mult, AL.add), [e_sq2])
                e_rr2 = b.op("vector", lambda e: e.reciprocal(stats[:, 21:22], stats[:, 20:21]), [e_ms2])
                e_r2 = b.op("scalar", lambda e: e.activation(
                    stats[:, 21:22], stats[:, 21:22], AF.Sqrt), [e_rr2])
                e_yn = b.op("vector", lambda e: e.tensor_scalar(
                    y_tile[:, :], y_tile[:, :], stats[:, 21:22], None, AL.mult, AL.bypass),
                    [e_r2])
                prev_ynw = b.dma(lambda e, bb=bb, cch=cch: e.dma_start(
                    out=dt["s_yn"][bb, cch * 128:(cch + 1) * 128, :], in_=y_tile[:, :]),
                    [e_yn])
                ev_ynw_prev = prev_ynw
                ev_xtok_use[par] = ev_y1mm[-1]
            yn_done = prev_ynw
            A.reset()
            A.reset()
            b.barrier()

            A.mark()
            ynTq = [A.alloc(BF16, 8192), A.alloc(BF16, 8192)]
            woutb = A.alloc(BF16, NT * D)
            ev_wo = []
            for r in range(16):
                ev_wo.append(b.dma(lambda e, r=r: e.dma_start(
                    out=woutb[:, r * D:(r + 1) * D],
                    in_=dt["s_wout"][r * 128:(r + 1) * 128, :]), [ev_wout]))
            ev_wof = fence()
            ev_yl_use = [None, None]
            for tb in range(4):
                par = tb % 2
                ev_yl0 = b.dmat(lambda e, bb=bb, tb=tb, par=par: e.dma_start_transpose(
                    ynTq[par][:, :].rearrange("p (dc s) -> p dc s", dc=16),
                    dt["s_yn"][bb, tb * 512:(tb + 1) * 512, :]),
                    [yn_done, ev_yl_use[par], b.ev("tensor")])
                ev_yl = fence()
                for ts4 in range(4):
                    tt = tb * 4 + ts4
                    e_hev = []
                    for et in range(2):
                        pi, pt, fe = ps.acquire()
                        mme = None
                        for dc in range(16):
                            mme = b.op("tensor", lambda e, p=pt, dc=dc, ts4=ts4, et=et, par=par: e.matmul(
                                p[:, 0:512],
                                ynTq[par][:, dc * 512 + ts4 * 128: dc * 512 + (ts4 + 1) * 128],
                                woutb[:, dc * D + et * 512: dc * D + (et + 1) * 512],
                                start=(dc == 0), stop=(dc == 15)),
                                [ev_yl, ev_wof, fe] if dc == 0 else [])
                        ev = b.op("vector", lambda e, p=pt, et=et: e.tensor_copy(
                            xln[:, et * 512:(et + 1) * 512], p[:, 0:512]),
                            [mme, prev_x3w, prev_xl3w])
                        ps.release(pi, ev)
                        e_hev.append(ev)
                        if ts4 == 3:
                            ev_yl_use[par] = mme
                    e_lnm = emit_ln(xln, D, lnt, e_hev)
                    evx2 = b.dma(lambda e, bb=bb, tt=tt: e.dma_start(
                        out=x0t[:, :], in_=dt["s_x2"][bb, tt * 128:(tt + 1) * 128, :]),
                        [b.ev("vector")])
                    evx2 = fence()
                    e_x3 = b.op("vector", lambda e: e.tensor_tensor(
                        d_t[:, 0:D], x0t[:, 0:D], lnt[:, 0:D], AL.add),
                        [e_lnm, evx2, prev_x3w])
                    ew1 = b.dma(lambda e, bb=bb, tt=tt: e.dma_start(
                        out=dt["s_x3"][bb, tt * 128:(tt + 1) * 128, :], in_=d_t[:, 0:D]), [e_x3])
                    x3_done.append(ew1)
                    prev_x3w = ew1
                    e_l3 = emit_ln(d_t, D, lnt, [e_x3])
                    e_lb3 = b.op("vector", lambda e: e.tensor_copy(ln_bf[:, 0:D], lnt[:, 0:D]),
                                 [e_l3, prev_xl3w])
                    ew2 = b.dma(lambda e, bb=bb, tt=tt: e.dma_start(
                        out=dt["s_xln3"][bb, tt * 128:(tt + 1) * 128, :], in_=ln_bf[:, 0:D]), [e_lb3])
                    x3_done.append(ew2)
                    prev_xl3w = ew2
            A.reset()
            b.barrier()
        tap_dram("x3", dt["s_x3"], [BPC, S, D], F32, x3_done)
        tap_dram("xln3", dt["s_xln3"], [BPC, S, D], BF16, x3_done)
        tap_dram("xs", dt["s_xs"], [BPC, DI, S], BF16, x3_done)
        tap_dram("zs", dt["s_z"], [BPC, S, DI], BF16, x3_done)
        tap_dram("yns", dt["s_yn"], [BPC, S, DI], BF16, x3_done)
        A.reset()
        b.barrier()

    if _PHASES >= 4:
        A.mark()
        w1b = A.alloc(BF16, 8 * 2048)
        w2b = A.alloc(BF16, 32 * D)
        hbuf = A.alloc(BF16, 32 * 256)
        xlq4 = [A.alloc(BF16, 2048), A.alloc(BF16, 2048)]
        ev_w2l = []
        for r in range(32):
            ev_w2l.append(b.dma(lambda e, r=r: e.dma_start(
                out=w2b[:, r * D:(r + 1) * D],
                in_=dt["s_tern2"][r * 128:(r + 1) * 128, :]), [ev_t2]))
        ev_w2f = fence()
        y_done = []
        prev_yw = None
        prev_lb4 = None
        ev_w1u = None
        ev_xl_use = [None, None]
        for bb in range(BPC):
            for tb in range(8):
                par = tb % 2
                ev_xl0 = b.dmat(lambda e, bb=bb, tb=tb, par=par: e.dma_start_transpose(
                    xlq4[par][:, :].rearrange("p (dc s) -> p dc s", dc=8),
                    dt["s_xln3"][bb, tb * 256:(tb + 1) * 256, :]),
                    [(x3_done[-1] if _PHASES >= 3 else None), ev_xl_use[par], b.ev("tensor")])
                ev_xl = fence()
                ev_h = {}
                mme = None
                for fh in range(2):
                    ev_w1l = []
                    for r in range(8):
                        ev_w1l.append(b.dma(lambda e, r=r, fh=fh: e.dma_start(
                            out=w1b[:, r * 2048:(r + 1) * 2048],
                            in_=dt["s_tern1"][r * 128:(r + 1) * 128, fh * 2048:(fh + 1) * 2048]),
                            [ev_t1, ev_w1u]))
                    ev_w1f = fence()
                    for fc in range(16):
                        fcg = fh * 16 + fc
                        pi, pt, fe = ps.acquire()
                        mme = None
                        for dc in range(8):
                            mme = b.op("tensor", lambda e, p=pt, dc=dc, fc=fc, par=par: e.matmul(
                                p[:, 0:256],
                                w1b[:, dc * 2048 + fc * 128: dc * 2048 + (fc + 1) * 128],
                                xlq4[par][:, dc * 256:(dc + 1) * 256],
                                start=(dc == 0), stop=(dc == 7)),
                                [ev_xl, ev_w1f, fe] if dc == 0 else [])
                        ev = b.op("scalar", lambda e, p=pt, fcg=fcg: e.activation(
                            hbuf[:, fcg * 256:(fcg + 1) * 256], p[:, 0:256], AF.Gelu,
                            bias=b1sb[:, fcg:fcg + 1]), [mme, evc[6], b.ev("tensor")])
                        ps.release(pi, ev)
                        ev_h[fcg] = ev
                    ev_w1u = mme
                for ts2 in range(2):
                    tt2 = tb * 2 + ts2
                    e_hev = []
                    for et in range(2):
                        pi, pt, fe = ps.acquire()
                        mme = None
                        for fcg in range(32):
                            mme = b.op("tensor", lambda e, p=pt, fcg=fcg, ts2=ts2, et=et: e.matmul(
                                p[:, 0:512],
                                hbuf[:, fcg * 256 + ts2 * 128: fcg * 256 + (ts2 + 1) * 128],
                                w2b[:, fcg * D + et * 512: fcg * D + (et + 1) * 512],
                                start=(fcg == 0), stop=False),
                                [ev_h[fcg], ev_w2f] + ([fe] if fcg == 0 else []))
                        mme = b.op("tensor", lambda e, p=pt, et=et: e.matmul(
                            p[:, 0:512], onesr_b[:, :], b2_b[:, et * 512:(et + 1) * 512],
                            start=False, stop=True), [ev_b2b, mme])
                        ev = b.op("vector", lambda e, p=pt, et=et: e.tensor_copy(
                            xln[:, et * 512:(et + 1) * 512], p[:, 0:512]),
                            [mme, prev_yw, prev_lb4])
                        ps.release(pi, ev)
                        e_hev.append(ev)
                    e_lnf = emit_ln(xln, D, lnt, e_hev)
                    evx3 = b.dma(lambda e, bb=bb, tt2=tt2: e.dma_start(
                        out=x0t[:, :], in_=dt["s_x3"][bb, tt2 * 128:(tt2 + 1) * 128, :]),
                        [b.ev("vector")])
                    evx3 = fence()
                    e_x4 = b.op("vector", lambda e: e.tensor_tensor(
                        d_t[:, 0:D], x0t[:, 0:D], lnt[:, 0:D], AL.add),
                        [e_lnf, evx3, prev_yw])
                    prev_lb4 = e_x4
                    ew = b.dma(lambda e, bb=bb, tt2=tt2: e.dma_start(
                        out=y_out[bb, tt2 * 128:(tt2 + 1) * 128, :], in_=d_t[:, 0:D]), [e_x4])
                    y_done.append(ew)
                    prev_yw = ew
        A.reset()

    if _PHASES < 4:
        A.mark()
        zt = A.alloc(F32, D)
        ez = b.op("vector", lambda e: e.memset(zt[:, 0:D], 0.0), [b.ev("sync")])
        last = None
        for bb in range(BPC):
            for tt in range(NT):
                last = b.dma(lambda e, bb=bb, tt=tt: e.dma_start(
                    out=y_out[bb, tt * 128:(tt + 1) * 128, :], in_=zt[:, 0:D]), [ez])
        A.reset()

    b.barrier()

    with nc.Block() as block:
        b.finalize(block, sems)

    return nc, keep, taps


_TAP_RESULTS = None


def kernel(**inputs):
    global _TAP_RESULTS
    f32 = np.float32

    def T(a):
        return np.ascontiguousarray(np.asarray(a, f32).T)

    W = {
        "wqT": T(inputs["L0_attn_wq"]), "wkT": T(inputs["L0_attn_wk"]),
        "wvT": T(inputs["L0_attn_wv"]), "woT": T(inputs["L0_attn_wo"]),
        "wfT": T(inputs["L0_ffn_wf"]), "ufT": T(inputs["L0_ffn_uf"]),
        "wiT": T(inputs["L0_ffn_wi"]), "uiT": T(inputs["L0_ffn_ui"]),
        "wcT": T(inputs["L0_ffn_wc"]), "ucT": T(inputs["L0_ffn_uc"]),
        "wo2T": T(inputs["L0_ffn_wo"]), "uoT": T(inputs["L0_ffn_uo"]),
        "winT": T(inputs["L1_attn_W_in"]), "woutT": T(inputs["L1_attn_W_out"]),
        "w1T": T(inputs["L1_ffn_W1"]), "w2T": T(inputs["L1_ffn_W2"]),
    }
    biases = {
        "bfc": np.ascontiguousarray(np.asarray(inputs["L0_ffn_bf"], f32).reshape(8, 128).T),
        "bic": np.ascontiguousarray(np.asarray(inputs["L0_ffn_bi"], f32).reshape(8, 128).T),
        "bcc": np.ascontiguousarray(np.asarray(inputs["L0_ffn_bc"], f32).reshape(8, 128).T),
        "incc": np.ascontiguousarray(np.asarray(inputs["L0_ffn_in_c"], f32).reshape(8, 128).T),
        "bor": np.asarray(inputs["L0_ffn_bo"], f32).reshape(1, D),
        "b1c": np.ascontiguousarray(np.asarray(inputs["L1_ffn_b1"], f32).reshape(32, 128).T),
        "b2r": np.asarray(inputs["L1_ffn_b2"], f32).reshape(1, D),
        "convw": np.ascontiguousarray(
            np.asarray(inputs["L1_attn_conv_w"], f32).reshape(18, 128, 4).transpose(1, 0, 2).reshape(128, 72)),
        "convb": np.ascontiguousarray(np.asarray(inputs["L1_attn_conv_b"], f32).reshape(18, 128).T),
        "dtb": np.asarray(inputs["L1_attn_dt_bias"], f32).reshape(32, 1),
        "alog": np.asarray(inputs["L1_attn_A_log"], f32).reshape(32, 1),
        "dvec": np.asarray(inputs["L1_attn_D"], f32).reshape(1, 32),
        "normw": np.ascontiguousarray(np.asarray(inputs["L1_attn_norm_w"], f32).reshape(16, 128).T),
    }
    consts = {
        "onesr": np.ones((1, 128), f32),
        "slnegA": np.ascontiguousarray(np.tril(np.full((128, 128), -1e30, f32), -1).T),
        "identw": np.eye(128, dtype=f32),
        "sltA": np.ascontiguousarray(np.tril(np.ones((128, 128), f32), -1)),
        "triuI": np.ascontiguousarray(np.triu(np.ones((128, 128), f32), 0)),
        "sel127": np.ascontiguousarray(
            np.outer((np.arange(128) == 127).astype(f32), np.ones(128, f32))),
    }
    x = np.asarray(inputs["L0_attn_x"], f32)

    nc, keep, taps = build_nc(tuple(_DBG))
    in_maps = []
    for c in range(NCORE):
        m = {"x0": np.ascontiguousarray(x[2 * c:2 * c + 2])}
        m.update(W)
        m.update(biases)
        m.update(consts)
        in_maps.append(m)
    res = run_bass_kernel_spmd(nc, in_maps, core_ids=list(range(NCORE)))
    _TAP_RESULTS = [{k: np.asarray(r["dbg_" + k]) for k in _DBG} for r in res.results]
    out = np.concatenate([np.asarray(r["y"]) for r in res.results], axis=0)
    return out.reshape(16, S, D)
